# version 03
# baseline (speedup 1.0000x reference)
import math
from contextlib import ExitStack

import numpy as np
import ml_dtypes
import concourse.bass as bass
import concourse.mybir as mybir
from concourse.bass_utils import run_bass_kernel_spmd

F32 = mybir.dt.float32
BF16 = mybir.dt.bfloat16
I32 = mybir.dt.int32
AF = mybir.ActivationFunctionType
ALU = mybir.AluOpType
NPBF = ml_dtypes.bfloat16

D = 2048
TC = 2048
S = 4096
B = 4
NCORE = 8
EPS = float(np.float32(1e-5))
CSHIFT = 12.0


class _Call:
    __slots__ = ("m", "a", "k", "chain")
    def __init__(self, m, a, k):
        self.m, self.a, self.k = m, a, k
        self.chain = []
    def then_inc(self, sem, n):
        self.chain.append((sem, n))
        return self

class _Rec:
    def __init__(self):
        self.calls = []
    def rec(self, m, *a, **k):
        c = _Call(m, a, k)
        self.calls.append(c)
        return c
    def _emit(self, eng):
        for c in self.calls:
            inst = getattr(eng, c.m)(*c.a, **c.k)
            for sem, n in c.chain:
                inst.then_inc(sem, n)

ENGS = ("sync", "tensor", "vector", "scalar", "gpsimd")

class Prog:
    def __init__(self, name):
        self.name = name
        self.nc = bass.Bass("TRN2", target_bir_lowering=False)
        self.ctx = ExitStack()
        self.q = {e: _Rec() for e in ENGS}
        self._sems = {}
        self.cnt = {}

    def sem(self, name):
        if name not in self._sems:
            self._sems[name] = self.ctx.enter_context(self.nc.semaphore(name))
            self.cnt[name] = 0
        return self._sems[name]

    def sbuf(self, name, shape, dtype):
        return self.ctx.enter_context(self.nc.sbuf_tensor(name, list(shape), dtype))

    def psum(self, name, shape, dtype=F32):
        return self.ctx.enter_context(self.nc.psum_tensor(name, list(shape), dtype))

    def din(self, name, shape, dtype):
        return self.nc.dram_tensor(name, list(shape), dtype, kind="ExternalInput").ap()

    def dout(self, name, shape, dtype):
        return self.nc.dram_tensor(name, list(shape), dtype, kind="ExternalOutput").ap()

    def dtmp(self, name, shape, dtype):
        return self.nc.dram_tensor(name, list(shape), dtype).ap()

    def inc(self, call, ch, n=1):
        call.then_inc(self.sem(ch), n)
        self.cnt[ch] += n
        return self.cnt[ch]

    def wait(self, eng, ch, val=None):
        v = self.cnt[ch] if val is None else val
        if v > 0:
            self.q[eng].rec("wait_ge", self.sem(ch), v)

    def at(self, ch):
        self.sem(ch)
        return self.cnt[ch]

    def dma(self, qname, out, in_, ch):
        c = self.q[qname].rec("dma_start", out=out, in_=in_)
        return self.inc(c, ch, 16)

    def finalize(self):
        nc = self.nc
        with self.ctx:
            with nc.Block() as block:
                for e in ENGS:
                    rec = self.q[e]
                    if rec.calls:
                        getattr(block, e)(lambda eng, rec=rec: rec._emit(eng))
        return nc


class PsumPool:
    def __init__(self, p: Prog, name, n):
        self.p = p
        self.tiles = [p.psum(f"{name}{i}", [128, 512]) for i in range(n)]
        self.free_at = [None] * n
        self.i = 0

    def get(self):
        i = self.i
        self.i = (self.i + 1) % len(self.tiles)
        fa = self.free_at[i]
        if fa is not None:
            self.p.wait("tensor", fa[0], fa[1])
            self.free_at[i] = None
        return self.tiles[i], i

    def done(self, i, ch, val=None):
        self.free_at[i] = (ch, self.p.at(ch) if val is None else val)



def emit_consts(p: Prog):
    t = {}
    t["ones128_bf"] = p.sbuf("ones128_bf", [128, 128], BF16)
    t["ones_col_bf"] = p.sbuf("ones_col_bf", [128, 1], BF16)
    t["ones_row_bf"] = p.sbuf("ones_row_bf", [1, 512], BF16)
    t["ones_row_T"] = p.sbuf("ones_row_T", [1, TC], BF16)
    p.q["vector"].rec("memset", t["ones_row_T"][:], 1.0)
    t["eps_col"] = p.sbuf("eps_col", [128, 1], F32)
    t["zero_col"] = p.sbuf("zero_col", [128, 1], F32)
    p.q["vector"].rec("memset", t["eps_col"][:], EPS)
    p.q["vector"].rec("memset", t["zero_col"][:], 0.0)
    p.q["vector"].rec("memset", t["ones128_bf"][:], 1.0)
    p.q["vector"].rec("memset", t["ones_col_bf"][:], 1.0)
    c = p.q["vector"].rec("memset", t["ones_row_bf"][:], 1.0)
    t["ready"] = ("dve", p.inc(c, "dve", 1))
    return t


def load_x_resident(p: Prog, x_dram, xsb, T, src_fp32=False):
    nch = D // 128
    if not src_fp32:
        for c in range(nch):
            v = p.dma("sync", xsb[c][:], x_dram[c * 128:(c + 1) * 128, :], "ldx")
        return ("ldx", v)
    stg = [p.sbuf(f"xstg{i}", [128, T], F32) for i in range(2)]
    for c in range(nch):
        s = stg[c % 2]
        if c >= 2:
            p.wait("sync", "cast", p.at("cast") - 1)
        v = p.dma("sync", s[:], x_dram[c * 128:(c + 1) * 128, :], "ldx")
        eng = "gpsimd" if c % 2 == 0 else "vector"
        p.wait(eng, "ldx", v)
        cc = p.q[eng].rec("tensor_copy", xsb[c][:], s[:])
        p.inc(cc, "cast", 1)
    return ("cast", p.at("cast"))


class LNStats:
    pass


def ln_stats(p: Prog, xsb, T, consts, pool: PsumPool, tag, x_ready=None, bufs=None,
             guard=None):
    nch = D // 128
    nblk = T // 512
    if bufs is None:
        st = LNStats()
        st.r_bc = p.sbuf(f"r_bc_{tag}", [128, T], F32)
        st.negmu_bf = p.sbuf(f"negmu_{tag}", [1, T], BF16)
        st.invr_bf = p.sbuf(f"invr_{tag}", [1, T], BF16)
        st.sq = [p.sbuf(f"sq_{tag}{i}", [128, 512], BF16) for i in range(2)]
        st.tmps = [p.sbuf(f"lntmp_{tag}{i}", [128, 512], F32) for i in range(3)]
    else:
        st = bufs
    sq, tmps = st.sq, st.tmps
    mub = tmps[2]
    p.wait("tensor", consts["ready"][0], consts["ready"][1])
    for eng in ("tensor", "scalar", "vector"):
        if x_ready is not None:
            p.wait(eng, x_ready[0], x_ready[1])
        if guard is not None:
            p.wait(eng, guard[0], guard[1])
    for blk in range(nblk):
        tmp = tmps[blk % 2]
        if blk >= 2:
            p.wait("vector", "act", p.at("act"))
        sl = slice(blk * 512, (blk + 1) * 512)
        sA, iA = pool.get()
        sB, iB = pool.get()
        for c in range(nch):
            k = (blk * nch + c)
            if k >= 2:
                p.wait("scalar", "pe", p.at("pe") - 1)
            cc = p.q["scalar"].rec("activation", sq[k % 2][:], xsb[c][:, sl], AF.Square)
            v = p.inc(cc, "act", 1)
            p.wait("tensor", "act", v)
            p.q["tensor"].rec("matmul", sA[:, :], consts["ones128_bf"][:], xsb[c][:, sl],
                              start=(c == 0), stop=(c == nch - 1), skip_group_check=True)
            cm = p.q["tensor"].rec("matmul", sB[:, :], consts["ones128_bf"][:], sq[k % 2][:, :],
                                   start=(c == 0), stop=(c == nch - 1), skip_group_check=True)
            p.inc(cm, "pe", 1)
        pe_v = p.at("pe")
        p.wait("vector", "pe", pe_v)
        c1 = p.q["vector"].rec("tensor_scalar", mub[:, :], sA[:, :], 1.0 / D, None, ALU.mult)
        c2 = p.q["vector"].rec("tensor_tensor", tmp[:, :], mub[:, :], mub[:, :], ALU.mult)
        c3 = p.q["vector"].rec("scalar_tensor_tensor", tmp[:, :], sB[:, :], 1.0 / D, tmp[:, :],
                               ALU.mult, ALU.subtract)
        c4 = p.q["vector"].rec("tensor_scalar", st.negmu_bf[:, sl], mub[0:1, :], -1.0, None, ALU.mult)
        v_dve = p.inc(c4, "dve", 1)
        pool.done(iA, "dve", v_dve)
        pool.done(iB, "dve", v_dve)
        p.wait("scalar", "dve", v_dve)
        c5 = p.q["scalar"].rec("activation", st.r_bc[:, sl], tmp[:, :], AF.Ln, bias=consts["eps_col"][:])
        c6 = p.q["scalar"].rec("activation", st.r_bc[:, sl], st.r_bc[:, sl], AF.Exp, scale=-0.5)
        c7 = p.q["scalar"].rec("activation", st.invr_bf[:, sl], tmp[0:1, :], AF.Sqrt, bias=consts["eps_col"][0:1, :])
        p.inc(c7, "act", 1)
    st.ready = ("act", p.at("act"))
    return st


def wsum_rowchunk(p, wt_bf, consts, ws_row, nch=16):
    guard = getattr(p, "_ws_guard", 0)
    if guard:
        p.wait("tensor", "dve", guard)
    for c in range(nch):
        cm = p.q["tensor"].rec("matmul", p._ws_ps[0:1, 0:128], consts["ones_col_bf"][:],
                               wt_bf[:, c * 128:(c + 1) * 128],
                               start=(c == 0), stop=(c == nch - 1), skip_group_check=True)
    v = p.inc(cm, "pe", 1)
    p.wait("vector", "pe", v)
    cc = p.q["vector"].rec("tensor_copy", ws_row[:], p._ws_ps[0:1, 0:128])
    p._ws_guard = p.inc(cc, "dve", 1)
    return p._ws_guard



class WStream:
    def __init__(self, p: Prog, tag, dram, n_chunks, dt_in, nrow=D, share_stg=None,
                 share_bufs=None, buf_w=None):
        self.p = p
        self.dram = dram
        self.n = n_chunks
        self.dt_in = dt_in
        self.nrow = nrow
        self.nch = nrow // 128
        self.tag = tag
        if share_bufs is not None:
            self.bufs = share_bufs.bufs
            self.last_use = share_bufs.last_use
        else:
            w = buf_w if buf_w is not None else self.nch * 128
            self.bufs = [p.sbuf(f"wbf_{tag}{i}", [128, w], BF16) for i in range(2)]
            self.last_use = [None, None]

    def load(self, o, slot, cast_eng=None):
        p = self.p
        blk = self.dram[o, :].rearrange("(c p j) -> p c j", p=128, j=128)
        lu = self.last_use[slot]
        q = "sync" if self.dt_in == BF16 else "gpsimd"
        if lu is not None:
            p.wait(q, lu[0], lu[1])
        v = p.dma(q, self.bufs[slot][:, 0:self.nch * 128].rearrange("p (c j) -> p c j", j=128),
                  blk, "ldw")
        return ("ldw", v)

    def mark_used(self, slot, ch, val):
        self.last_use[slot] = (ch, val)


def projection(p, xsb, T, ws: WStream, consts, pool, st, evac, n_out_chunks=None,
               rank1=(), cast_eng="gpsimd", x_ready=None):
    nblk = T // 512
    n_o = ws.n if n_out_chunks is None else n_out_chunks
    ws_row = p.sbuf(f"wsrow_{ws.tag}", [1, 128], BF16)
    if x_ready is not None:
        p.wait("tensor", x_ready[0], x_ready[1])
    ready = ws.load(0, 0, cast_eng)
    for o in range(n_o):
        slot = o % 2
        nslot = 1 - slot
        if o + 1 < n_o:
            nxt = ws.load(o + 1, nslot, cast_eng)
        wt = ws.bufs[slot]
        p.wait("tensor", ready[0], ready[1])
        if st is not None:
            wsum_rowchunk(p, wt, consts, ws_row, nch=ws.nch)
        for blk in range(nblk):
            sl = slice(blk * 512, (blk + 1) * 512)
            ps, pi = pool.get()
            n_extra = (1 if st is not None else 0) + len(rank1)
            for c in range(ws.nch):
                cm = p.q["tensor"].rec("matmul", ps[:, :], wt[:, c * 128:(c + 1) * 128],
                                       xsb[c][:, sl], start=(c == 0),
                                       stop=(c == ws.nch - 1 and n_extra == 0),
                                       skip_group_check=True)
            done_cnt = 0
            if st is not None:
                done_cnt += 1
                cm = p.q["tensor"].rec("matmul", ps[:, :], ws_row[:], st.negmu_bf[:, sl],
                                       start=False, stop=(done_cnt == n_extra),
                                       skip_group_check=True)
            for (lf, rrow) in rank1:
                done_cnt += 1
                cm = p.q["tensor"].rec("matmul", ps[:, :], lf(o), rrow[0:1, sl],
                                       start=False, stop=(done_cnt == n_extra),
                                       skip_group_check=True)
            v = p.inc(cm, "pe", 1)
            mil = evac(o, blk, ps, v)
            pool.done(pi, mil[0], mil[1])
        ws.mark_used(slot, *mil)
        if o + 1 < n_o:
            ready = nxt



def build_launch1():
    p = Prog("l1")
    x_in = p.din("x0", [D, TC], F32)
    w_in = p.din("wqkvT", [48, D * 128], F32)
    x_out = p.dout("x_bf", [D, TC], BF16)
    qkv_out = p.dout("qkv", [3 * D, TC], BF16)

    consts = emit_consts(p)
    pool = PsumPool(p, "pp", 6)
    p._ws_ps = p.psum("ws_ps", [1, 512])
    xsb = [p.sbuf(f"xr{c}", [128, TC], BF16) for c in range(16)]
    xrdy = load_x_resident(p, x_in, xsb, TC, src_fp32=True)
    p.wait("gpsimd", xrdy[0], xrdy[1])
    for c in range(16):
        p.dma("gpsimd", x_out[c * 128:(c + 1) * 128, :], xsb[c][:], "sto")

    st = ln_stats(p, xsb, TC, consts, pool, "a", x_ready=xrdy)
    ws = WStream(p, "qkv", w_in, 48, F32)
    ob = [p.sbuf(f"ob{i}", [128, 512], BF16) for i in range(4)]

    def evac(o, blk, ps, pe_v):
        t = ob[(o * 4 + blk) % 4]
        p.wait("vector", "pe", pe_v)
        if (o * 4 + blk) >= 4:
            p.wait("vector", "sto", p.at("sto") - 3 * 16)
        cc = p.q["vector"].rec("tensor_tensor", t[:], ps[:, :], st.r_bc[:, blk * 512:(blk + 1) * 512], ALU.mult)
        v = p.inc(cc, "dve", 1)
        p.wait("gpsimd", "dve", v)
        p.dma("gpsimd", qkv_out[o * 128:(o + 1) * 128, blk * 512:(blk + 1) * 512], t[:], "sto")
        return ("dve", v)

    p.wait("tensor", st.ready[0], st.ready[1])
    projection(p, xsb, TC, ws, consts, pool, st, evac)
    p.wait("gpsimd", "sto")
    return p.finalize()



def build_attn_launch(dh, HH, name):
    p = Prog(name)
    qf = p.din("qf", [HH * dh, S], BF16)
    kf = p.din("kf", [HH * dh, S], BF16)
    vt = p.din("vt", [S, HH * dh], BF16)
    masks = p.din("masks", [4, 128, 512], BF16)
    o_n = p.dout("o_n", [HH * dh, S], BF16)

    consts = emit_consts(p)
    pool = PsumPool(p, "pp", 6)
    ps_o = p.psum("ps_o", [128, 512])
    ps_r = p.psum("ps_r", [1, 512])
    sh_col = p.sbuf("sh_col", [128, 1], F32)
    p.q["vector"].rec("memset", sh_col[:], -float(CSHIFT))
    msb = p.sbuf("msb", [128, 4 * 512], BF16)
    qsb = [p.sbuf(f"qsb{i}", [dh, S], BF16) for i in range(2)]
    ksb = [p.sbuf(f"ksb{i}", [dh, S], BF16) for i in range(2)]
    vsb = [p.sbuf(f"vsb{i}", [128, 32 * dh], BF16) for i in range(2)]
    psb = [p.sbuf(f"psb{i}", [128, 512], BF16) for i in range(4)]
    osb = [p.sbuf(f"osb{i}", [dh, 512], BF16) for i in range(2)]
    rsb = [p.sbuf(f"rsb{i}", [1, 512], F32) for i in range(2)]
    ivb = [p.sbuf(f"ivb{i}", [128, 512], F32) for i in range(2)]
    onesf = p.sbuf("onesf", [1, 128], F32)
    p.q["vector"].rec("memset", onesf[:], 1.0)
    mrd = p.dma("sync", msb[:].rearrange("p (m j) -> p m j", j=512),
                masks[:].rearrange("m p j -> p m j"), "ldx")
    cmemset = p.q["vector"].rec("nop")
    vmem = p.inc(cmemset, "dve", 1)
    p.wait("scalar", "dve", vmem)
    p.wait("vector", "ldx", mrd)
    p.wait("tensor", consts["ready"][0], consts["ready"][1])

    def load_head(h, slot, guard):
        if guard is not None:
            p.wait("sync", guard[0], guard[1])
        p.dma("sync", qsb[slot][:], qf[h * dh:(h + 1) * dh, :], "ldx")
        p.dma("sync", ksb[slot][:], kf[h * dh:(h + 1) * dh, :], "ldx")
        v = p.dma("sync", vsb[slot][:].rearrange("p (j d) -> p j d", d=dh),
                  vt[:, h * dh:(h + 1) * dh].rearrange("(j p) d -> p j d", p=128), "ldx")
        return ("ldx", v)

    rdy = load_head(0, 0, None)
    pk = 0
    pvals = [None] * 4
    ovals = [None] * 2
    rvals = [None] * 2
    ev_guard = None
    for h in range(HH):
        slot = h % 2
        nxt = load_head(h + 1, 1 - slot, ev_guard) if h + 1 < HH else None
        p.wait("tensor", rdy[0], rdy[1])
        for tq in range(8):
            sl = slice(tq * 512, (tq + 1) * 512)
            ntk = (tq + 1) * 4
            if ev_guard is not None:
                p.wait("tensor", ev_guard[0], ev_guard[1])
            first_pv = True
            for tk in range(ntk):
                ps_s, pi = pool.get()
                cm = p.q["tensor"].rec("matmul", ps_s[:, :], ksb[slot][:, tk * 128:(tk + 1) * 128],
                                       qsb[slot][:, sl], start=True, stop=True,
                                       skip_group_check=True)
                v_s = p.inc(cm, "pe", 1)
                pslot = pk % 4
                pt = psb[pslot]
                pk += 1
                if pvals[pslot] is not None:
                    p.wait("scalar", "pe", pvals[pslot])
                p.wait("scalar", "pe", v_s)
                ca = p.q["scalar"].rec("activation", pt[:], ps_s[:, :], AF.Exp,
                                       bias=sh_col[:], scale=1.0 / math.sqrt(dh))
                v_e = p.inc(ca, "act", 1)
                pool.done(pi, "act", v_e)
                dk = tk * 128 - tq * 512
                if dk >= 0:
                    m = dk // 128
                    p.wait("vector", "act", v_e)
                    cmask = p.q["vector"].rec("tensor_tensor", pt[:], pt[:],
                                              msb[:, m * 512:(m + 1) * 512], ALU.mult)
                    v_e2 = p.inc(cmask, "dve", 1)
                    p.wait("tensor", "dve", v_e2)
                else:
                    p.wait("tensor", "act", v_e)
                p.q["tensor"].rec("matmul", ps_o[0:dh, :], vsb[slot][:, tk * dh:(tk + 1) * dh],
                                  pt[:], start=first_pv, stop=False, skip_group_check=True)
                cr = p.q["tensor"].rec("matmul", ps_r[:, :], consts["ones_col_bf"][:],
                                       pt[:], start=first_pv, stop=(tk == ntk - 1),
                                       skip_group_check=True)
                first_pv = False
                v_pvr = p.inc(cr, "pe", 1)
                pvals[pslot] = v_pvr
            k2 = (h * 8 + tq) % 2
            ot, rt, iv = osb[k2], rsb[k2], ivb[k2]
            p.wait("vector", "pe", v_pvr)
            if rvals[k2] is not None:
                p.wait("vector", "pe", rvals[k2][0])
            c1 = p.q["vector"].rec("tensor_copy", rt[:], ps_r[:, :])
            v1 = p.inc(c1, "dve", 1)
            p.wait("scalar", "dve", v1)
            p.q["scalar"].rec("activation", rt[:], rt[:], AF.Ln)
            c2 = p.q["scalar"].rec("activation", rt[:], rt[:], AF.Exp, scale=-1.0)
            v2 = p.inc(c2, "act", 1)
            ps_i, pii = pool.get()
            p.wait("tensor", "act", v2)
            c3 = p.q["tensor"].rec("matmul", ps_i[:, :], onesf[:], rt[:],
                                   start=True, stop=True, skip_group_check=True)
            v3 = p.inc(c3, "pe", 1)
            rvals[k2] = (v3, None)
            if ovals[k2] is not None:
                p.wait("scalar", "dve", ovals[k2][0])
            p.wait("scalar", "pe", v3)
            c4 = p.q["scalar"].rec("copy", iv[:], ps_i[:, :])
            v4 = p.inc(c4, "act", 1)
            pool.done(pii, "act", v4)
            p.wait("vector", "act", v4)
            if ovals[k2] is not None:
                p.wait("vector", "sto", ovals[k2][1])
            c5 = p.q["vector"].rec("tensor_tensor", ot[:], ps_o[0:dh, :], iv[0:dh, :], ALU.mult)
            v5 = p.inc(c5, "dve", 1)
            ev_guard = ("dve", v5)
            p.wait("gpsimd", "dve", v5)
            v_st = p.dma("gpsimd", o_n[h * dh:(h + 1) * dh, sl], ot[:], "sto")
            ovals[k2] = (p.at("dve"), v_st)
        if nxt is not None:
            rdy = nxt
    p.wait("gpsimd", "sto")
    return p.finalize()




def _load_rows(p, dst_bf, src_f32ap, n):
    if not hasattr(p, "_rowstg"):
        p._rowstg = p.sbuf("rowstg", [1, 512], F32)
        p._rowstg_free = None
    stg = p._rowstg
    for j in range(0, n, 512):
        w = min(512, n - j)
        if p._rowstg_free is not None:
            p.wait("sync", p._rowstg_free[0], p._rowstg_free[1])
        v = p.dma("sync", stg[0:1, 0:w], src_f32ap[0:1, j:j + w], "ldx")
        p.wait("vector", "ldx", v)
        cc = p.q["vector"].rec("tensor_copy", dst_bf[0:1, j:j + w], stg[0:1, 0:w])
        p._rowstg_free = ("dve", p.inc(cc, "dve", 1))
    return p._rowstg_free


def build_launch_A(name, Do, ffn):
    I, wdt = ffn["I"], ffn["dt"]
    p = Prog(name)
    x_in = p.din("x", [D, TC], BF16)
    o_in = p.din("o_in", [Do, TC], BF16)
    woT = p.din("woT", [16, Do * 128], F32)
    has_bias = ffn.get("bias", False)
    if has_bias:
        brow_d = p.din("brow", [1, D], F32)
    wg_d = p.din("wg", [I // 128, D * 128], wdt)
    wu_d = p.din("wu", [I // 128, D * 128], wdt)
    gsc_d = p.din("gsc", [128, I // 128], F32)
    usc_d = p.din("usc", [128, I // 128], F32)
    x_out = p.dout("x_up", [D, TC], BF16)
    h_out = p.dout("h", [I, TC], BF16)

    consts = emit_consts(p)
    pool = PsumPool(p, "pp", 6)
    p._ws_ps = p.psum("ws_ps", [1, 512])
    xsb = [p.sbuf(f"xr{c}", [128, TC], BF16) for c in range(16)]
    osb = [p.sbuf(f"oi{c}", [128, TC], BF16) for c in range(Do // 128)]
    xrdy = load_x_resident(p, x_in, xsb, TC)
    for c in range(Do // 128):
        ov = p.dma("sync", osb[c][:], o_in[c * 128:(c + 1) * 128, :], "ldx")
    ordy = ("ldx", ov)
    gsc = p.sbuf("gsc_s", [128, I // 128], F32)
    usc = p.sbuf("usc_s", [128, I // 128], F32)
    p.dma("sync", gsc[:], gsc_d[:], "ldx")
    scv = p.dma("sync", usc[:], usc_d[:], "ldx")
    rank1 = []
    if has_bias:
        brow = p.sbuf("brow_bf", [1, D], BF16)
        _load_rows(p, brow, brow_d[:], D)
        rank1 = [(lambda o, brow=brow: brow[0:1, o * 128:(o + 1) * 128], None)]

    wo_ws = WStream(p, "wo", woT, 16, F32, nrow=Do)
    p.wait("vector", xrdy[0], xrdy[1])
    if rank1:
        p.wait("tensor", "dve", p.at("dve"))

    def evac_res(o, blk, ps, pe_v):
        sl = slice(blk * 512, (blk + 1) * 512)
        p.wait("vector", "pe", pe_v)
        cc = p.q["vector"].rec("tensor_tensor", xsb[o][:, sl], ps[:, :], xsb[o][:, sl], ALU.add)
        return ("dve", p.inc(cc, "dve", 1))

    rank1b = [(rank1[0][0], consts["ones_row_T"])] if rank1 else []
    projection(p, osb, TC, wo_ws, consts, pool, None, evac_res,
               rank1=rank1b, x_ready=ordy)
    v_res = p.at("dve")
    p.wait("gpsimd", "dve", v_res)
    for c in range(16):
        p.dma("gpsimd", x_out[c * 128:(c + 1) * 128, :], xsb[c][:], "sto")

    st = ln_stats(p, xsb, TC, consts, pool, "f", x_ready=("dve", v_res))

    wg_ws = WStream(p, "wg", wg_d, I // 128, wdt)
    wu_ws = WStream(p, "wu", wu_d, I // 128, wdt, share_stg=wg_ws)
    wsg = p.sbuf("wsg", [1, 128], BF16)
    wsu = p.sbuf("wsu", [1, 128], BF16)
    tfg0 = p.sbuf("tfg0", [128, 512], F32)
    tfg = [tfg0, tfg0]
    tfu0 = p.sbuf("tfu0", [128, 512], F32)
    tfu = [tfu0, tfu0]
    gys0 = p.sbuf("gys0", [128, 512], BF16)
    gys = [gys0, gys0]
    uys0 = p.sbuf("uys0", [128, 512], BF16)
    uys = [uys0, uys0]
    hys = [p.sbuf(f"hys{i}", [128, 512], BF16) for i in range(2)]
    guard_g = [None, None]
    guard_h = [None, None]
    guard_s = [None, None]
    p.wait("tensor", st.ready[0], st.ready[1])
    p.wait("scalar", "ldx", scv)
    rg = wg_ws.load(0, 0, "gpsimd")
    ru = wu_ws.load(0, 0, "vector")
    for o in range(I // 128):
        slot = o % 2
        if o + 1 < I // 128:
            nrg = wg_ws.load(o + 1, 1 - slot, "gpsimd")
            nru = wu_ws.load(o + 1, 1 - slot, "vector")
        p.wait("tensor", rg[0], rg[1])
        p.wait("tensor", ru[0], ru[1])
        wg_buf = wg_ws.bufs[slot]
        wu_buf = wu_ws.bufs[slot]
        wsum_rowchunk(p, wg_buf, consts, wsg)
        vsu = wsum_rowchunk(p, wu_buf, consts, wsu)
        for blk in range(4):
            sl = slice(blk * 512, (blk + 1) * 512)
            k2 = (o * 4 + blk) % 2
            psG, piG = pool.get()
            psU, piU = pool.get()
            for c in range(16):
                p.q["tensor"].rec("matmul", psG[:, :], wg_buf[:, c * 128:(c + 1) * 128],
                                  xsb[c][:, sl], start=(c == 0), stop=False,
                                  skip_group_check=True)
            p.q["tensor"].rec("matmul", psG[:, :], wsg[:], st.negmu_bf[0:1, sl],
                              start=False, stop=True, skip_group_check=True)
            for c in range(16):
                p.q["tensor"].rec("matmul", psU[:, :], wu_buf[:, c * 128:(c + 1) * 128],
                                  xsb[c][:, sl], start=(c == 0), stop=False,
                                  skip_group_check=True)
            cm = p.q["tensor"].rec("matmul", psU[:, :], wsu[:], st.negmu_bf[0:1, sl],
                                   start=False, stop=True, skip_group_check=True)
            pe_v = p.inc(cm, "pe", 1)
            p.wait("vector", "pe", pe_v)
            if guard_g[0] is not None:
                p.wait("vector", "act", guard_g[0])
            c1 = p.q["vector"].rec("tensor_tensor", tfg[0][:], psG[:, :],
                                   st.r_bc[:, sl], ALU.mult)
            v1 = p.inc(c1, "dve", 1)
            p.wait("scalar", "dve", v1)
            if guard_h[0] is not None:
                p.wait("scalar", "dve", guard_h[0])
            c2 = p.q["scalar"].rec("activation", gys[0][:], tfg[0][:], AF.Silu,
                                   scale=gsc[:, o:o + 1])
            v2 = p.inc(c2, "act", 1)
            guard_g[0] = v2
            c3 = p.q["vector"].rec("tensor_tensor", tfu[0][:], psU[:, :],
                                   st.r_bc[:, sl], ALU.mult)
            c4 = p.q["vector"].rec("tensor_scalar", uys[0][:], tfu[0][:],
                                   usc[:, o:o + 1], None, ALU.mult)
            p.inc(c4, "dve", 1)
            pool.done(piG, "act", v2)
            pool.done(piU, "dve", p.at("dve"))
            p.wait("vector", "act", v2)
            if guard_s[k2] is not None:
                p.wait("vector", "sto", guard_s[k2])
            c5 = p.q["vector"].rec("tensor_tensor", hys[k2][:], gys[0][:], uys[0][:], ALU.mult)
            v5 = p.inc(c5, "dve", 1)
            guard_h[0] = v5
            p.wait("gpsimd", "dve", v5)
            guard_s[k2] = p.dma("gpsimd", h_out[o * 128:(o + 1) * 128, sl], hys[k2][:], "sto")
        wg_ws.mark_used(slot, "dve", v5)
        wu_ws.mark_used(slot, "dve", v5)
        if o + 1 < I // 128:
            rg, ru = nrg, nru
    p.wait("gpsimd", "sto")
    return p.finalize()


def build_launch_B(name, ffn, proj):
    I, wdt = ffn["I"], ffn["dt"]
    nsub = I // 512
    p = Prog(name)
    x_in = p.din("x", [D, TC], BF16)
    h_in = p.din("h", [I, TC], BF16)
    wd_d = p.din("wd", [64, (I // 4) * 128], wdt)
    dsc_d = p.din("dsc", [128, 16], F32)
    final = ffn.get("final", False)
    x_out = p.dout("x_fin" if final else "x_up", [D, TC], F32 if final else BF16)
    consts = emit_consts(p)
    pool = PsumPool(p, "pp", 5)
    p._ws_ps = p.psum("ws_ps", [1, 512])
    ps_hs = p.psum("ps_hs", [1, 512])
    xsb = [p.sbuf(f"xr{c}", [128, TC], BF16) for c in range(16)]
    scrw = (4 * nsub + 15) // 16 * 512
    scr = [p.sbuf(f"hs{c}", [128, scrw], BF16) for c in range(16)]
    xrdy = load_x_resident(p, x_in, xsb, TC)
    dsc = p.sbuf("dsc_s", [128, 16], F32)
    dscv = p.dma("sync", dsc[:], dsc_d[:], "ldx")
    p.wait("vector", "ldx", dscv)
    p.wait("vector", xrdy[0], xrdy[1])
    p.wait("tensor", consts["ready"][0], consts["ready"][1])
    off128 = ffn.get("offset128", False)
    hsb = p.sbuf("hsb", [128, 512], F32)
    hrow = p.sbuf("hrow", [1, 512], F32)
    onesf = p.sbuf("onesf", [1, 128], F32)
    cmf = p.q["vector"].rec("memset", onesf[:], 1.0)
    p.inc(cmf, "dve", 1)
    wd_ws = WStream(p, "wd", wd_d, 64, wdt, nrow=I // 4, buf_w=2048)
    fsb = [p.sbuf(f"fsb{i}", [128, 512], F32) for i in range(2)] if final else None
    tdn = p.sbuf("tdn", [128, 512], F32)
    fguard = [None, None]

    scr_guard = None
    nxt = wd_ws.load(0, 0, "gpsimd")
    for t in range(4):
        tsl = slice(t * 512, (t + 1) * 512)
        if scr_guard is not None:
            p.wait("sync", scr_guard[0], scr_guard[1])
        for g in range(4 * nsub):
            hv = p.dma("sync", scr[g % 16][:, (g // 16) * 512:(g // 16) * 512 + 512],
                       h_in[g * 128:(g + 1) * 128, tsl], "ldh")
        p.wait("tensor", "ldh", hv)
        if off128:
            for g in range(4 * nsub):
                chs = p.q["tensor"].rec("matmul", ps_hs[:, :], consts["ones_col_bf"][:],
                                        scr[g % 16][:, (g // 16) * 512:(g // 16) * 512 + 512],
                                        start=(g == 0), stop=(g == 4 * nsub - 1),
                                        skip_group_check=True)
            vhs = p.inc(chs, "pe", 1)
            p.wait("vector", "pe", vhs)
            ch1 = p.q["vector"].rec("tensor_copy", hrow[:], ps_hs[:, :])
            vh1 = p.inc(ch1, "dve", 1)
            psB, piB = pool.get()
            p.wait("tensor", "dve", vh1)
            ch2 = p.q["tensor"].rec("matmul", psB[:, :], onesf[:], hrow[:],
                                    start=True, stop=True, skip_group_check=True)
            vh2 = p.inc(ch2, "pe", 1)
            p.wait("scalar", "pe", vh2)
            p.wait("scalar", "dve", p.at("dve"))
            ch3 = p.q["scalar"].rec("copy", hsb[:], psB[:, :])
            vh3 = p.inc(ch3, "act", 1)
            pool.done(piB, "act", vh3)
            p.wait("vector", "act", vh3)
        for d in range(16):
            ps, pi = pool.get()
            for q in range(4):
                bi = d * 4 + q
                rwd = nxt
                slot = bi % 2
                if bi + 1 < 64:
                    nxt = wd_ws.load(bi + 1, (bi + 1) % 2, "gpsimd")
                elif t + 1 < 4:
                    nxt = wd_ws.load(0, 0, "gpsimd")
                p.wait("tensor", rwd[0], rwd[1])
                buf = wd_ws.bufs[slot]
                for ic in range(nsub):
                    g = q * nsub + ic
                    cm = p.q["tensor"].rec("matmul", ps[:, :], buf[:, ic * 128:(ic + 1) * 128],
                                           scr[g % 16][:, (g // 16) * 512:(g // 16) * 512 + 512],
                                           start=(bi % 4 == 0 and ic == 0),
                                           stop=(q == 3 and ic == nsub - 1),
                                           skip_group_check=True)
                pe_v = p.inc(cm, "pe", 1)
                wd_ws.mark_used(slot, "pe", pe_v)
            p.wait("vector", "pe", pe_v)
            if final:
                k2 = d % 2
                if fguard[k2] is not None:
                    p.wait("vector", "sto", fguard[k2])
                if off128:
                    p.q["vector"].rec("scalar_tensor_tensor", fsb[k2][:], hsb[:], -128.0,
                                      ps[:, :], ALU.mult, ALU.add)
                    p.q["vector"].rec("tensor_scalar", fsb[k2][:], fsb[k2][:],
                                      dsc[:, d:d + 1], None, ALU.mult)
                else:
                    p.q["vector"].rec("tensor_scalar", fsb[k2][:], ps[:, :],
                                      dsc[:, d:d + 1], None, ALU.mult)
                c3 = p.q["vector"].rec("tensor_tensor", fsb[k2][:], fsb[k2][:],
                                       xsb[d][:, tsl], ALU.add)
                v3 = p.inc(c3, "dve", 1)
                pool.done(pi, "dve", v3)
                p.wait("gpsimd", "dve", v3)
                fguard[k2] = p.dma("gpsimd", x_out[d * 128:(d + 1) * 128, tsl], fsb[k2][:], "sto")
            else:
                if off128:
                    p.q["vector"].rec("scalar_tensor_tensor", tdn[:], hsb[:], -128.0,
                                      ps[:, :], ALU.mult, ALU.add)
                    c2 = p.q["vector"].rec("scalar_tensor_tensor", xsb[d][:, tsl], tdn[:],
                                           dsc[:, d:d + 1], xsb[d][:, tsl], ALU.mult, ALU.add)
                else:
                    c2 = p.q["vector"].rec("scalar_tensor_tensor", xsb[d][:, tsl], ps[:, :],
                                           dsc[:, d:d + 1], xsb[d][:, tsl], ALU.mult, ALU.add)
                v3 = p.inc(c2, "dve", 1)
                pool.done(pi, "dve", v3)
        scr_guard = ("pe", p.at("pe"))
    v_down = p.at("dve")
    if final:
        p.wait("gpsimd", "sto")
        return p.finalize()

    p.wait("gpsimd", "dve", v_down)
    for c in range(16):
        p.dma("gpsimd", x_out[c * 128:(c + 1) * 128, :], xsb[c][:], "sto")

    if proj is None:
        p.wait("gpsimd", "sto")
        return p.finalize()

    st = ln_stats(p, xsb, TC, consts, pool, "b", x_ready=("dve", v_down))
    ncat = proj["n"]
    wcat_d = p.din("wcat", [ncat, D * 128], F32)
    kinds = proj["kinds"]
    outs = {}
    for (nm, nchk) in proj["outs"]:
        outs[nm] = p.dout(nm, [nchk * 128, TC], BF16)
    rank1 = []
    if proj.get("bias"):
        brow_d = p.din("brow", [1, ncat * 128], F32)
        brow = p.sbuf("brow_bf", [1, ncat * 128], BF16)
        _load_rows(p, brow, brow_d[:], ncat * 128)
        rank1 = [(lambda o, brow=brow: brow[0:1, o * 128:(o + 1) * 128], st.invr_bf)]
        p.wait("tensor", "dve", p.at("dve"))
    if proj.get("rope"):
        cosd = p.din("cosT", [128, TC], BF16)
        sind = p.din("sinT", [128, TC], BF16)
        r2d = p.din("rot2", [128, 128], BF16)
        cosT = p.sbuf("cosT_s", [128, TC], BF16)
        sinT = p.sbuf("sinT_s", [128, TC], BF16)
        r2 = p.sbuf("r2_s", [128, 128], BF16)
        p.dma("sync", cosT[:], cosd[:], "ldx")
        p.dma("sync", sinT[:], sind[:], "ldx")
        vrp = p.dma("sync", r2[:], r2d[:], "ldx")
        p.wait("tensor", "ldx", vrp)
        p.wait("vector", "ldx", vrp)
        ps_rot = p.psum("ps_rot", [128, 512])
    t1buf = None
    if proj.get("gk"):
        t1buf = p.sbuf("t1buf", [128, TC], BF16)
    ws = WStream(p, "wc", wcat_d, ncat, F32, share_bufs=wd_ws)
    eb = [p.sbuf(f"eb{i}", [128, 512], BF16) for i in range(2)]
    ef0 = p.sbuf("ef0", [128, 512], F32)
    ef = [ef0, ef0]
    evg = [None] * 2
    rot_state = [None]

    chunk_map = []
    for (nm, nchk) in proj["outs"]:
        for j in range(nchk):
            chunk_map.append((nm, j))

    def evac(o, blk, ps, pe_v):
        sl = slice(blk * 512, (blk + 1) * 512)
        kind = kinds[o]
        k3 = (o * 4 + blk) % 2
        bt, ft = eb[k3], ef[k3]
        p.wait("vector", "pe", pe_v)
        if evg[k3] is not None:
            p.wait("vector", evg[k3][0], evg[k3][1])
        if kind == "gkt1":
            cc = p.q["vector"].rec("tensor_tensor", t1buf[:, sl], ps[:, :],
                                   st.r_bc[:, sl], ALU.mult)
            v = p.inc(cc, "dve", 1)
            evg[k3] = None
            return ("dve", v)
        if kind in ("rope_q", "rope_k"):
            c1 = p.q["vector"].rec("tensor_tensor", bt[:], ps[:, :], st.r_bc[:, sl], ALU.mult)
            v1 = p.inc(c1, "dve", 1)
            if rot_state[0] is not None:
                p.wait("tensor", rot_state[0][0], rot_state[0][1])
            p.wait("tensor", "dve", v1)
            c2 = p.q["tensor"].rec("matmul", ps_rot[:, :], r2[:], bt[:],
                                   start=True, stop=True, skip_group_check=True)
            v2 = p.inc(c2, "pe", 1)
            p.wait("vector", "pe", v2)
            c3 = p.q["vector"].rec("tensor_tensor", ft[:], ps_rot[:, :], sinT[:, sl], ALU.mult)
            v3a = p.inc(c3, "dve", 1)
            rot_state[0] = ("dve", v3a)
            p.q["vector"].rec("tensor_tensor", bt[:], bt[:], cosT[:, sl], ALU.mult)
            c5 = p.q["vector"].rec("tensor_tensor", bt[:], bt[:], ft[:], ALU.add)
            v5 = p.inc(c5, "dve", 1)
            nm, j = chunk_map[o]
            p.wait("gpsimd", "dve", v5)
            sv = p.dma("gpsimd", outs[nm][j * 128:(j + 1) * 128, sl], bt[:], "sto")
            evg[k3] = ("sto", sv)
            return ("dve", v5)
        if kind == "plain":
            cc = p.q["vector"].rec("tensor_tensor", bt[:], ps[:, :], st.r_bc[:, sl], ALU.mult)
            v = p.inc(cc, "dve", 1)
            mil = ("dve", v)
        else:
            if getattr(p, "_efg", None) is not None:
                p.wait("vector", "act", p._efg)
            c1 = p.q["vector"].rec("tensor_tensor", ft[:], ps[:, :], st.r_bc[:, sl], ALU.mult)
            v1 = p.inc(c1, "dve", 1)
            p.wait("scalar", "dve", v1)
            if evg[k3] is not None:
                p.wait("scalar", evg[k3][0], evg[k3][1])
            fn = AF.Relu if kind == "relu" else AF.Silu
            sc = (1.0 / math.sqrt(128.0)) if kind == "relu" else 1.0
            c2 = p.q["scalar"].rec("activation", bt[:], ft[:], fn, scale=sc)
            v = p.inc(c2, "act", 1)
            p._efg = v
            mil = ("act", v)
        nm, j = chunk_map[o]
        p.wait("gpsimd", mil[0], mil[1])
        sv = p.dma("gpsimd", outs[nm][j * 128:(j + 1) * 128, sl], bt[:], "sto")
        evg[k3] = ("sto", sv)
        return mil

    projection(p, xsb, TC, ws, consts, pool, st, evac, rank1=rank1)

    if proj.get("gk"):
        _emit_gk(p, t1buf, st, consts, pool)
    p.wait("gpsimd", "sto")
    return p.finalize()


def _emit_gk(p, t1buf, st, consts, pool):
    wgk2_d = p.din("wgk2", [16, D], F32)
    bgk2_d = p.din("bgk2", [1, D], F32)
    gk_out = p.dout("gk_ls", [TC, D], F32)
    w2s = p.sbuf("w2s", [16, D], F32)
    w2b = p.sbuf("w2b", [16, D], BF16)
    v = p.dma("sync", w2s[:], wgk2_d[:], "ldx")
    p.wait("vector", "ldx", v)
    cw = p.q["vector"].rec("tensor_copy", w2b[:], w2s[:])
    vw = p.inc(cw, "dve", 1)
    brs = p.sbuf("bgk_row", [1, D], F32)
    bbc = p.sbuf("bgk_bc", [128, D], BF16)
    vb = p.dma("sync", brs[:], bgk2_d[:], "ldx")
    onesf2 = p.sbuf("onesf2", [1, 128], F32)
    one_col = p.sbuf("one_col", [128, 1], F32)
    cmem = p.q["vector"].rec("memset", onesf2[:], 1.0)
    cmo = p.q["vector"].rec("memset", one_col[:], 1.0)
    vmem = p.inc(cmo, "dve", 1)
    p.wait("tensor", "ldx", vb)
    p.wait("tensor", "dve", vmem)
    for nb in range(4):
        psB, piB = pool.get()
        cb = p.q["tensor"].rec("matmul", psB[:, :], onesf2[:], brs[0:1, nb * 512:(nb + 1) * 512],
                               start=True, stop=True, skip_group_check=True)
        vb2 = p.inc(cb, "pe", 1)
        p.wait("scalar", "pe", vb2)
        cb2 = p.q["scalar"].rec("copy", bbc[:, nb * 512:(nb + 1) * 512], psB[:, :])
        vb3 = p.inc(cb2, "act", 1)
        pool.done(piB, "act", vb3)
    p.wait("tensor", "dve", vw)
    p.wait("vector", "act", p.at("act"))
    ga = [p.sbuf(f"ga{i}", [128, 512], F32) for i in range(2)]
    gb = [p.sbuf(f"gb{i}", [128, 512], F32) for i in range(2)]
    gg = [None, None]
    p.wait("tensor", "dve", p.at("dve"))
    for tc_ in range(16):
        for nb in range(4):
            nsl = slice(nb * 512, (nb + 1) * 512)
            ps, pi = pool.get()
            cm = p.q["tensor"].rec("matmul", ps[:, :], t1buf[0:16, tc_ * 128:(tc_ + 1) * 128],
                                   w2b[:, nsl], start=True, stop=True, skip_group_check=True)
            pe_v = p.inc(cm, "pe", 1)
            k2 = (tc_ * 4 + nb) % 2
            at, btl = ga[k2], gb[k2]
            p.wait("vector", "pe", pe_v)
            if gg[k2] is not None:
                p.wait("vector", "sto", gg[k2])
            c1 = p.q["vector"].rec("scalar_tensor_tensor", at[:], ps[:, :], 1.0,
                                   bbc[:, nsl], ALU.mult, ALU.add)
            v1 = p.inc(c1, "dve", 1)
            pool.done(pi, "dve", v1)
            p.wait("scalar", "dve", v1)
            c2 = p.q["scalar"].rec("activation", btl[:], at[:], AF.Abs)
            va = p.inc(c2, "act", 1)
            c3 = p.q["scalar"].rec("activation", btl[:], btl[:], AF.Exp, scale=-1.0)
            c4 = p.q["scalar"].rec("activation", btl[:], btl[:], AF.Ln, bias=one_col[:])
            v4 = p.inc(c4, "act", 1)
            p.wait("vector", "act", va)
            p.q["vector"].rec("tensor_scalar", at[:], at[:], 0.0, None, ALU.min)
            p.wait("vector", "act", v4)
            c6 = p.q["vector"].rec("tensor_tensor", at[:], at[:], btl[:], ALU.subtract)
            v6 = p.inc(c6, "dve", 1)
            p.wait("gpsimd", "dve", v6)
            gg[k2] = p.dma("gpsimd", gk_out[tc_ * 128:(tc_ + 1) * 128, nsl], at[:], "sto")




def build_lin_attn_launch(name):
    HH, dh = 8, 128
    p = Prog(name)
    kt = p.din("kt", [S, HH * dh], BF16)
    vt2 = p.din("vt2", [S, HH * dh], BF16)
    qf = p.din("qf", [HH * dh, S], BF16)
    uf = p.din("uf", [HH * dh, S], BF16)
    og = p.dout("og", [HH * dh, S], BF16)
    consts = emit_consts(p)
    pool = PsumPool(p, "pp", 6)
    ps_kv = p.psum("ps_kv", [128, 128])
    ps_s2 = p.psum("ps_s2", [1, 512])
    onesf = p.sbuf("onesf", [1, 128], F32)
    cm0 = p.q["vector"].rec("memset", onesf[:], 1.0)
    p.inc(cm0, "dve", 1)
    ksb = [p.sbuf(f"ksb{i}", [128, 32 * 128], BF16) for i in range(2)]
    vsb = [p.sbuf(f"vsb{i}", [128, 32 * 128], BF16) for i in range(2)]
    qsb = [p.sbuf(f"qsb{i}", [128, S], BF16) for i in range(2)]
    usb = [p.sbuf(f"usb{i}", [128, S], BF16) for i in range(2)]
    kvb = p.sbuf("kvb", [128, 128], BF16)
    o2b = p.sbuf("o2b", [128, 512], BF16)
    srow = p.sbuf("srow", [1, 512], F32)
    rbs = [p.sbuf(f"rbs{i}", [128, 512], F32) for i in range(2)]
    otf = [p.sbuf(f"otf{i}", [128, 512], F32) for i in range(2)]
    otb = [p.sbuf(f"otb{i}", [128, 512], BF16) for i in range(2)]
    p.wait("tensor", consts["ready"][0], consts["ready"][1])

    def load_head(h, slot, guard):
        if guard is not None:
            p.wait("sync", guard[0], guard[1])
        p.dma("sync", ksb[slot][:].rearrange("p (j d) -> p j d", d=128),
              kt[:, h * dh:(h + 1) * dh].rearrange("(j p) d -> p j d", p=128), "ldx")
        p.dma("sync", vsb[slot][:].rearrange("p (j d) -> p j d", d=128),
              vt2[:, h * dh:(h + 1) * dh].rearrange("(j p) d -> p j d", p=128), "ldx")
        p.dma("sync", qsb[slot][:], qf[h * dh:(h + 1) * dh, :], "ldx")
        v = p.dma("sync", usb[slot][:], uf[h * dh:(h + 1) * dh, :], "ldx")
        return ("ldx", v)

    rdy = load_head(0, 0, None)
    guards = [None, None]
    kv_guard = None
    for h in range(HH):
        slot = h % 2
        nxt = load_head(h + 1, 1 - slot, kv_guard) if h + 1 < HH else None
        p.wait("tensor", rdy[0], rdy[1])
        if kv_guard is not None:
            p.wait("tensor", kv_guard[0], kv_guard[1])
        for j in range(32):
            cm = p.q["tensor"].rec("matmul", ps_kv[:, :], ksb[slot][:, j * 128:(j + 1) * 128],
                                   vsb[slot][:, j * 128:(j + 1) * 128],
                                   start=(j == 0), stop=(j == 31), skip_group_check=True)
        vkv = p.inc(cm, "pe", 1)
        p.wait("vector", "pe", vkv)
        ckv = p.q["vector"].rec("tensor_copy", kvb[:], ps_kv[:, :])
        vkvc = p.inc(ckv, "dve", 1)
        p.wait("tensor", "dve", vkvc)
        for tq in range(8):
            sl = slice(tq * 512, (tq + 1) * 512)
            k2 = tq % 2
            ps_o, pio = pool.get()
            cm1 = p.q["tensor"].rec("matmul", ps_o[:, :], kvb[:], qsb[slot][:, sl],
                                    start=True, stop=True, skip_group_check=True)
            v1 = p.inc(cm1, "pe", 1)
            p.wait("scalar", "pe", v1)
            if guards[k2] is not None:
                p.wait("scalar", guards[k2][0], guards[k2][1])
            c2 = p.q["scalar"].rec("activation", o2b[:], ps_o[:, :], AF.Square)
            v2 = p.inc(c2, "act", 1)
            p.wait("tensor", "act", v2)
            cm3 = p.q["tensor"].rec("matmul", ps_s2[:, :], consts["ones_col_bf"][:], o2b[:],
                                    start=True, stop=True, skip_group_check=True)
            v3 = p.inc(cm3, "pe", 1)
            p.wait("vector", "pe", v3)
            c4 = p.q["vector"].rec("tensor_scalar", srow[:], ps_s2[:, :], 1.0 / dh, 1e-24,
                                   ALU.mult, ALU.max)
            v4 = p.inc(c4, "dve", 1)
            p.wait("scalar", "dve", v4)
            p.q["scalar"].rec("activation", srow[:], srow[:], AF.Ln)
            c5 = p.q["scalar"].rec("activation", srow[:], srow[:], AF.Exp, scale=-0.5)
            v5 = p.inc(c5, "act", 1)
            ps_bb, pib = pool.get()
            p.wait("tensor", "act", v5)
            c6 = p.q["tensor"].rec("matmul", ps_bb[:, :], onesf[:], srow[:],
                                   start=True, stop=True, skip_group_check=True)
            v6 = p.inc(c6, "pe", 1)
            p.wait("scalar", "pe", v6)
            c7 = p.q["scalar"].rec("copy", rbs[k2][:], ps_bb[:, :])
            v7 = p.inc(c7, "act", 1)
            pool.done(pib, "act", v7)
            p.wait("vector", "act", v7)
            c8 = p.q["vector"].rec("tensor_tensor", otf[k2][:], ps_o[:, :], rbs[k2][:], ALU.mult)
            c9 = p.q["vector"].rec("tensor_tensor", otb[k2][:], otf[k2][:], usb[slot][:, sl], ALU.mult)
            v9 = p.inc(c9, "dve", 1)
            pool.done(pio, "dve", v9)
            p.wait("gpsimd", "dve", v9)
            sv = p.dma("gpsimd", og[h * dh:(h + 1) * dh, sl], otb[k2][:], "sto")
            guards[k2] = ("sto", sv)
        kv_guard = ("dve", v9)
        if nxt is not None:
            rdy = nxt
    p.wait("gpsimd", "sto")
    return p.finalize()



def build_gla_launch(name):
    p = Prog(name)
    qf8 = p.din("qf8", [1024, S], BF16)
    kf8 = p.din("kf8", [1024, S], BF16)
    kt8 = p.din("kt8", [S, 1024], BF16)
    vt8 = p.din("vt8", [S, 512], BF16)
    gkt = p.din("gkt", [S, 1024], F32)
    gt = p.din("gt", [S, 512], BF16)
    wnb = p.din("wnorm_bc", [128, 256], F32)
    trd = p.din("trilm", [64, 64], F32)
    usd = p.din("ustrict", [64, 64], F32)
    mad = p.din("maskA", [64, 64], BF16)
    og8 = p.dout("og8", [S, 512], BF16)
    consts = emit_consts(p)
    pool = PsumPool(p, "pp", 5)
    ps_b = p.psum("ps_b", [128, 512])
    wn = p.sbuf("wn", [128, 256], F32)
    tri = p.sbuf("tri", [64, 64], F32)
    ust = p.sbuf("ust", [64, 64], F32)
    mka = p.sbuf("mka", [64, 64], BF16)
    lnsc = p.sbuf("lnsc", [128, 1], F32)
    p.q["vector"].rec("memset", lnsc[:], float(np.log(1.0 / np.sqrt(512.0))))
    p.dma("sync", wn[:], wnb[:], "ldx")
    p.dma("sync", tri[:], trd[:], "ldx")
    p.dma("sync", ust[:], usd[:], "ldx")
    vcl = p.dma("sync", mka[:], mad[:], "ldx")
    p.wait("tensor", "ldx", vcl)
    p.wait("tensor", consts["ready"][0], consts["ready"][1])
    p.wait("vector", "ldx", vcl)

    qt = [p.sbuf(f"qt{i}", [128, 512], BF16) for i in range(2)]
    ktl = [p.sbuf(f"ktl{i}", [128, 512], BF16) for i in range(2)]
    ktm = [p.sbuf(f"ktm{i}", [64, 512], BF16) for i in range(2)]
    vtm = [p.sbuf(f"vtm{i}", [64, 256], BF16) for i in range(2)]
    gkm = [p.sbuf(f"gkm{i}", [64, 512], F32) for i in range(2)]
    gtm = [p.sbuf(f"gtm{i}", [64, 256], BF16) for i in range(2)]
    Sf = [p.sbuf(f"Sf{i}", [128, 256], F32) for i in range(4)]
    Sb = [p.sbuf(f"Sb{i}", [128, 256], BF16) for i in range(4)]
    ebq = p.sbuf("ebq", [128, 256], F32)
    ebk = p.sbuf("ebk", [128, 256], F32)
    ebl = p.sbuf("ebl", [128, 4], F32)
    qe = p.sbuf("qe", [128, 256], BF16)
    kd = p.sbuf("kd", [128, 256], BF16)
    erev = p.sbuf("erev", [64, 512], F32)
    ku = p.sbuf("ku", [64, 512], BF16)
    atb = p.sbuf("atb", [64, 64], BF16)
    osq = p.sbuf("osq", [64, 256], BF16)
    s2c = p.sbuf("s2c", [64, 1], F32)
    otf = p.sbuf("otf", [64, 256], F32)
    gsl = p.sbuf("gsl", [64, 256], BF16)
    oob = [p.sbuf(f"oob{i}", [64, 256], BF16) for i in range(2)]
    oguard = [None, None]

    def load_pair(h, cp, slot, guard):
        if guard is not None:
            p.wait("sync", guard[0], guard[1])
        rows = slice(cp * 128, (cp + 1) * 128)
        for dkc in range(4):
            p.dma("sync", qt[slot][:, dkc * 128:(dkc + 1) * 128],
                  qf8[h * 512 + dkc * 128: h * 512 + (dkc + 1) * 128, rows], "ldx")
            v = p.dma("sync", ktl[slot][:, dkc * 128:(dkc + 1) * 128],
                      kf8[h * 512 + dkc * 128: h * 512 + (dkc + 1) * 128, rows], "ldx")
        return ("ldx", v)

    def load_chunk(h, c, slot, guard):
        if guard is not None:
            p.wait("sync", guard[0], guard[1])
        rows = slice(c * 64, (c + 1) * 64)
        p.dma("sync", ktm[slot][:], kt8[rows, h * 512:(h + 1) * 512], "ldx")
        p.dma("sync", vtm[slot][:], vt8[rows, h * 256:(h + 1) * 256], "ldx")
        p.dma("sync", gkm[slot][:], gkt[rows, h * 512:(h + 1) * 512], "ldx")
        v = p.dma("sync", gtm[slot][:], gt[rows, h * 256:(h + 1) * 256], "ldx")
        return ("ldx", v)

    pb_guard = None
    sb_ready = None
    for h in range(2):
        p.wait("vector", "dve", p.at("dve"))
        for i in range(4):
            p.q["vector"].rec("memset", Sf[i][:], 0.0)
            cz = p.q["vector"].rec("memset", Sb[i][:], 0.0)
        vz = p.inc(cz, "dve", 1)
        sb_ready = ("dve", vz)
        if h == 0:
            rdy = load_pair(0, 0, 0, None)
            crdy = load_chunk(0, 0, 0, None)
        else:
            rdy, crdy = rdy_h1, crdy_h1
        for c in range(64):
            cp, sub = c // 2, c % 2
            slot = cp % 2
            cslot = c % 2
            ss = slice(sub * 64, (sub + 1) * 64)
            if c > 0:
                crdy = cnxt
            if c + 1 < 64:
                cnxt = load_chunk(h, c + 1, 1 - cslot, ("dve", p.at("dve")))
            elif h == 0:
                cnxt = load_chunk(1, 0, 1 - cslot, ("dve", p.at("dve")))
            else:
                cnxt = None
            if sub == 0:
                if cp > 0:
                    rdy = nxt
                if cp + 1 < 32:
                    nxt = load_pair(h, cp + 1, 1 - slot, ("dve", p.at("dve")))
                elif h == 0:
                    nxt = load_pair(1, 0, 1 - slot, ("dve", p.at("dve")))
                else:
                    nxt = None
                p.wait("tensor", rdy[0], rdy[1])
            p.wait("tensor", crdy[0], crdy[1])
            if pb_guard is not None:
                p.wait("tensor", "act", pb_guard)
            for dkc in range(4):
                p.q["tensor"].rec("matmul", ps_b[:, dkc * 64:(dkc + 1) * 64],
                                  gkm[cslot][:, dkc * 128:(dkc + 1) * 128], tri[:],
                                  start=True, stop=True, skip_group_check=True)
            psr, pir = pool.get()
            cr = p.q["tensor"].rec("matmul", psr[0:64, 0:512], ust[:], gkm[cslot][:, :],
                                   start=True, stop=True, skip_group_check=True)
            vb = p.inc(cr, "pe", 1)
            p.wait("scalar", "pe", vb)
            p.wait("scalar", "dve", p.at("dve"))
            p.q["scalar"].rec("activation", ebq[:], ps_b[:, 0:256], AF.Exp,
                              scale=1.0 / 16.0, bias=lnsc[:])
            p.q["scalar"].rec("activation", ebk[:], ps_b[:, 0:256], AF.Exp, scale=-1.0 / 16.0)
            p.q["scalar"].rec("activation", erev[:], psr[0:64, 0:512], AF.Exp, scale=1.0 / 16.0)
            for dkc in range(4):
                c4 = p.q["scalar"].rec("activation", ebl[:, dkc:dkc + 1],
                                       ps_b[:, dkc * 64 + 63: dkc * 64 + 64], AF.Exp,
                                       scale=1.0 / 16.0)
            v4 = p.inc(c4, "act", 1)
            pb_guard = v4
            pool.done(pir, "act", v4)
            p.wait("vector", "act", v4)
            for dkc in range(4):
                bs = slice(dkc * 64, (dkc + 1) * 64)
                qs = slice(dkc * 128 + sub * 64, dkc * 128 + sub * 64 + 64)
                p.q["vector"].rec("tensor_tensor", qe[:, bs], qt[slot][:, qs], ebq[:, bs], ALU.mult)
                p.q["vector"].rec("tensor_tensor", kd[:, bs], ktl[slot][:, qs], ebk[:, bs], ALU.mult)
            c5 = p.q["vector"].rec("tensor_tensor", ku[:], ktm[cslot][:, :], erev[:], ALU.mult)
            v5 = p.inc(c5, "dve", 1)
            psA, piA = pool.get()
            p.wait("tensor", "dve", v5)
            for dkc in range(4):
                bs = slice(dkc * 64, (dkc + 1) * 64)
                cA = p.q["tensor"].rec("matmul", psA[0:64, 0:64], kd[:, bs], qe[:, bs],
                                       start=(dkc == 0), stop=(dkc == 3), skip_group_check=True)
            vA = p.inc(cA, "pe", 1)
            p.wait("vector", "pe", vA)
            cM = p.q["vector"].rec("tensor_tensor", atb[:], psA[0:64, 0:64], mka[:], ALU.mult)
            vM = p.inc(cM, "dve", 1)
            pool.done(piA, "dve", vM)
            ps_o, pio = pool.get()
            p.wait("tensor", "dve", vM)
            if sb_ready is not None:
                p.wait("tensor", sb_ready[0], sb_ready[1])
            p.q["tensor"].rec("matmul", ps_o[0:64, 0:256], atb[:], vtm[cslot][:, :],
                              start=True, stop=False, skip_group_check=True)
            for dkc in range(4):
                bs = slice(dkc * 64, (dkc + 1) * 64)
                cO = p.q["tensor"].rec("matmul", ps_o[0:64, 0:256], qe[:, bs], Sb[dkc][:],
                                       start=False, stop=(dkc == 3), skip_group_check=True)
            vO = p.inc(cO, "pe", 1)
            su_tiles = []
            for dkc in range(4):
                psu, piu = pool.get()
                cU = p.q["tensor"].rec("matmul", psu[0:128, 0:256], ku[:, dkc * 128:(dkc + 1) * 128],
                                       vtm[cslot][:, :], start=True, stop=True,
                                       skip_group_check=True)
                su_tiles.append((psu, piu))
            vU = p.inc(cU, "pe", 1)
            p.wait("scalar", "pe", vO)
            csq = p.q["scalar"].rec("activation", osq[:], ps_o[0:64, 0:256], AF.Square,
                                    accum_out=s2c[:])
            vsq = p.inc(csq, "act", 1)
            p.wait("vector", "act", vsq)
            cv1 = p.q["vector"].rec("tensor_scalar", s2c[:], s2c[:], 1.0 / 256.0, EPS,
                                    ALU.mult, ALU.add)
            vv1 = p.inc(cv1, "dve", 1)
            p.wait("scalar", "dve", vv1)
            p.q["scalar"].rec("activation", s2c[:], s2c[:], AF.Ln)
            cs2 = p.q["scalar"].rec("activation", s2c[:], s2c[:], AF.Exp, scale=-0.5)
            csg = p.q["scalar"].rec("activation", gsl[:], gtm[cslot][:, :], AF.Silu)
            vs2 = p.inc(csg, "act", 1)
            k2 = c % 2
            p.wait("vector", "act", vs2)
            if oguard[k2] is not None:
                p.wait("vector", "sto", oguard[k2])
            p.q["vector"].rec("tensor_scalar", otf[:], ps_o[0:64, 0:256], s2c[:, 0:1], None, ALU.mult)
            p.q["vector"].rec("tensor_tensor", otf[:], otf[:], wn[0:64, :], ALU.mult)
            co = p.q["vector"].rec("tensor_tensor", oob[k2][:], otf[:], gsl[:], ALU.mult)
            vo2 = p.inc(co, "dve", 1)
            pool.done(pio, "dve", vo2)
            p.wait("gpsimd", "dve", vo2)
            oguard[k2] = p.dma("gpsimd", og8[c * 64:(c + 1) * 64,
                                              h * 256:(h + 1) * 256], oob[k2][:], "sto")
            p.wait("vector", "pe", vU)
            for dkc in range(4):
                psu, piu = su_tiles[dkc]
                p.q["vector"].rec("scalar_tensor_tensor", Sf[dkc][:], Sf[dkc][:],
                                  ebl[:, dkc:dkc + 1], psu[0:128, 0:256], ALU.mult, ALU.add)
                cS = p.q["vector"].rec("tensor_copy", Sb[dkc][:], Sf[dkc][:])
                vS = p.inc(cS, "dve", 1)
                pool.done(piu, "dve", vS)
            sb_ready = ("dve", vS)
            if sub == 0:
                pass
        if h == 0:
            rdy_h1, crdy_h1 = nxt, cnxt
    p.wait("gpsimd", "sto")
    return p.finalize()



def _pretile(Wt):
    din, dout = Wt.shape
    assert dout % 128 == 0
    blocks = Wt.reshape(din, dout // 128, 128).transpose(1, 0, 2).reshape(dout // 128, din * 128)
    return np.ascontiguousarray(blocks)


def _causal_masks():
    m = np.zeros((4, 128, 512), np.float32)
    for k in range(4):
        pp = np.arange(128)[:, None] + k * 128
        jj = np.arange(512)[None, :]
        m[k] = (pp <= jj)
    return m.astype(NPBF)


def _rope_tables():
    dh = 64
    theta = 10000.0 ** (-np.arange(0, dh, 2, dtype=np.float64) / dh)
    pos = np.arange(S, dtype=np.float64)
    tid = np.floor((np.arange(S) // 576) * 0.5).astype(np.float64)
    ang_half = (pos + tid)[:, None] * theta[None, :]
    ang = np.concatenate([ang_half, ang_half], axis=1)
    cosT = np.cos(ang).T.astype(np.float32)
    sinT = np.sin(ang).T.astype(np.float32)
    cos128 = np.concatenate([cosT, cosT], axis=0)
    sin128 = np.concatenate([sinT, sinT], axis=0)
    R = np.zeros((64, 64), np.float32)
    for j in range(32):
        R[32 + j, j] = -1.0
        R[j, 32 + j] = 1.0
    R2 = np.zeros((128, 128), np.float32)
    R2[0:64, 0:64] = R
    R2[64:128, 64:128] = R
    return cos128.astype(NPBF), sin128.astype(NPBF), R2.astype(NPBF)


import os as _os
_PROFILE = bool(_os.environ.get("MK_PROFILE"))
_PROFILE_ONLY = _os.environ.get("MK_PROFILE_ONLY")
_RUN_IDX = [0]

def _run(nc, in_maps):
    i = _RUN_IDX[0]
    _RUN_IDX[0] += 1
    tr = _PROFILE or (_PROFILE_ONLY is not None and i == int(_PROFILE_ONLY))
    if tr:
        import time as _t
        t0 = _t.time()
        r = run_bass_kernel_spmd(nc, in_maps, core_ids=list(range(NCORE)), trace=True)
        print(f"[prof] #{i}: exec={r.exec_time_ns and r.exec_time_ns/1e3:.0f}us "
              f"wall={_t.time()-t0:.1f}s", flush=True)
        for c in range(NCORE):
            for k, v in r.results[c].items():
                a = np.asarray(v, np.float32)
                bad = int(np.isnan(a).sum()) + int(np.isinf(a).sum())
                if bad:
                    print(f"[prof] #{i} core{c}: OUTPUT {k} HAS {bad} bad vals", flush=True)
        return r.results
    return run_bass_kernel_spmd(nc, in_maps, core_ids=list(range(NCORE))).results


def _halves(cores_arrs, axis=1):
    return [np.concatenate([cores_arrs[2 * b], cores_arrs[2 * b + 1]], axis=axis)
            for b in range(B)]


def kernel(**inputs):
    f32 = np.float32
    xs = inputs["L0_attn_a_hidden_states"]
    x_fm = [np.ascontiguousarray(xs[c // 2, (c % 2) * TC:(c % 2 + 1) * TC, :].T)
            for c in range(NCORE)]
    masks = _causal_masks()
    cos128, sin128, R2 = _rope_tables()

    wqkvT = _pretile(np.ascontiguousarray(inputs["L0_attn_a_w_qkv"].T))
    nc = build_launch1()
    r1 = _run(nc, [{"x0": x_fm[c], "wqkvT": wqkvT} for c in range(NCORE)])

    Qb = _halves([r1[c]["qkv"][0:D] for c in range(NCORE)])
    Kb = _halves([r1[c]["qkv"][D:2 * D] for c in range(NCORE)])
    Vtb = [np.ascontiguousarray(q.T) for q in
           _halves([r1[c]["qkv"][2 * D:3 * D] for c in range(NCORE)])]
    nc = build_attn_launch(128, 8, "l2")
    in2 = []
    for c in range(NCORE):
        b, hg = c // 2, c % 2
        sl = slice(hg * 1024, (hg + 1) * 1024)
        in2.append({"qf": np.ascontiguousarray(Qb[b][sl]),
                    "kf": np.ascontiguousarray(Kb[b][sl]),
                    "vt": np.ascontiguousarray(Vtb[b][:, sl]),
                    "masks": masks})
    r2 = _run(nc, in2)

    ffnL0 = dict(I=8192, dt=I32)
    woT_a = _pretile(np.ascontiguousarray(inputs["L0_attn_a_w_o"].T))
    wgL0 = _pretile(np.ascontiguousarray(inputs["L0_ffn_gate_w"].T))
    wuL0 = _pretile(np.ascontiguousarray(inputs["L0_ffn_up_w"].T))
    gscL0 = np.ascontiguousarray(inputs["L0_ffn_gate_scale"].reshape(64, 128).T).astype(f32)
    uscL0 = np.ascontiguousarray(inputs["L0_ffn_up_scale"].reshape(64, 128).T).astype(f32)
    Ob = [np.concatenate([r2[2 * b]["o_n"], r2[2 * b + 1]["o_n"]], axis=0) for b in range(B)]
    nc = build_launch_A("l3a", D, ffnL0)
    in3 = [{"x": r1[c]["x_bf"], "o_in": np.ascontiguousarray(Ob[c // 2][:, (c % 2) * TC:(c % 2 + 1) * TC]),
            "woT": woT_a, "wg": wgL0, "wu": wuL0, "gsc": gscL0, "usc": uscL0}
           for c in range(NCORE)]
    r3a = _run(nc, in3)

    wdL0 = _pretile_down(np.ascontiguousarray(inputs["L0_ffn_down_w"].T), 8192)
    dscL0 = np.ascontiguousarray(inputs["L0_ffn_down_scale"].reshape(16, 128).T).astype(f32)
    wcat3 = np.concatenate([
        _pretile(np.ascontiguousarray(inputs["L0_attn_b_Wq"].T)),
        _pretile(np.ascontiguousarray(inputs["L0_attn_b_Wk"].T)),
        _pretile(np.ascontiguousarray(inputs["L0_attn_b_Wv"].T))], axis=0)
    brow3 = np.concatenate([inputs["L0_attn_b_bq"], inputs["L0_attn_b_bk"],
                            inputs["L0_attn_b_bv"]])[None, :].astype(f32)
    proj3 = dict(n=48, kinds=(["rope_q"] * 16 + ["rope_k"] * 16 + ["plain"] * 16),
                 outs=[("qb", 16), ("kb", 16), ("vb", 16)], bias=True, rope=True)
    nc = build_launch_B("l3b", dict(I=8192, dt=I32, offset128=True), proj3)
    in3b = []
    for c in range(NCORE):
        tok = slice((c % 2) * TC, (c % 2 + 1) * TC)
        in3b.append({"x": r3a[c]["x_up"], "h": r3a[c]["h"], "wd": wdL0, "dsc": dscL0,
                     "wcat": wcat3, "brow": brow3,
                     "cosT": np.ascontiguousarray(cos128[:, tok]),
                     "sinT": np.ascontiguousarray(sin128[:, tok]),
                     "rot2": R2})
    r3b = _run(nc, in3b)

    Qb = _halves([r3b[c]["qb"] for c in range(NCORE)])
    Kb = _halves([r3b[c]["kb"] for c in range(NCORE)])
    Vtb = [np.ascontiguousarray(q.T) for q in _halves([r3b[c]["vb"] for c in range(NCORE)])]
    nc = build_attn_launch(64, 16, "l4")
    in4 = []
    for c in range(NCORE):
        b, hg = c // 2, c % 2
        sl = slice(hg * 1024, (hg + 1) * 1024)
        in4.append({"qf": np.ascontiguousarray(Qb[b][sl]),
                    "kf": np.ascontiguousarray(Kb[b][sl]),
                    "vt": np.ascontiguousarray(Vtb[b][:, sl]),
                    "masks": masks})
    r4 = _run(nc, in4)

    woT_b = _pretile(np.ascontiguousarray(inputs["L0_attn_b_Wo"].T))
    Ob = [np.concatenate([r4[2 * b]["o_n"], r4[2 * b + 1]["o_n"]], axis=0) for b in range(B)]
    ffnL0b = dict(I=8192, dt=I32, bias=True)
    nc = build_launch_A("l5a", D, ffnL0b)
    in5 = [{"x": r3b[c]["x_up"], "o_in": np.ascontiguousarray(Ob[c // 2][:, (c % 2) * TC:(c % 2 + 1) * TC]),
            "woT": woT_b, "brow": inputs["L0_attn_b_bo"][None, :].astype(f32),
            "wg": wgL0, "wu": wuL0, "gsc": gscL0, "usc": uscL0}
           for c in range(NCORE)]
    r5a = _run(nc, in5)

    wcat5 = np.concatenate([
        _pretile(np.ascontiguousarray(inputs["L1_attn_a_wq"].T)),
        _pretile(np.ascontiguousarray(inputs["L1_attn_a_wk"].T)),
        _pretile(np.ascontiguousarray(inputs["L1_attn_a_wv"].T)),
        _pretile(np.ascontiguousarray(inputs["L1_attn_a_wu"].T))], axis=0)
    proj5 = dict(n=64, kinds=(["relu"] * 32 + ["plain"] * 16 + ["silu"] * 16),
                 outs=[("qo", 16), ("ko", 16), ("vo", 16), ("uo", 16)])
    nc = build_launch_B("l5b", dict(I=8192, dt=I32, offset128=True), proj5)
    in5b = [{"x": r5a[c]["x_up"], "h": r5a[c]["h"], "wd": wdL0, "dsc": dscL0, "wcat": wcat5}
            for c in range(NCORE)]
    r5b = _run(nc, in5b)

    Qb = _halves([r5b[c]["qo"] for c in range(NCORE)])
    Kb = _halves([r5b[c]["ko"] for c in range(NCORE)])
    Vb = _halves([r5b[c]["vo"] for c in range(NCORE)])
    Ub = _halves([r5b[c]["uo"] for c in range(NCORE)])
    nc = build_lin_attn_launch("l6")
    in6 = []
    for c in range(NCORE):
        b, hg = c // 2, c % 2
        sl = slice(hg * 1024, (hg + 1) * 1024)
        in6.append({"kt": np.ascontiguousarray(Kb[b][sl].T),
                    "vt2": np.ascontiguousarray(Vb[b][sl].T),
                    "qf": np.ascontiguousarray(Qb[b][sl]),
                    "uf": np.ascontiguousarray(Ub[b][sl])})
    r6 = _run(nc, in6)

    woT_l1a = _pretile(np.ascontiguousarray(inputs["L1_attn_a_wo"].T))
    wgL1 = _pretile(np.ascontiguousarray(inputs["L1_ffn_gate_w"].T))
    wuL1 = _pretile(np.ascontiguousarray(inputs["L1_ffn_up_w"].T))
    gscL1 = np.ascontiguousarray(inputs["L1_ffn_gate_s"].reshape(44, 128).T).astype(f32)
    uscL1 = np.ascontiguousarray(inputs["L1_ffn_up_s"].reshape(44, 128).T).astype(f32)
    ffnL1 = dict(I=5632, dt=F32)
    Ob = [np.concatenate([r6[2 * b]["og"], r6[2 * b + 1]["og"]], axis=0) for b in range(B)]
    nc = build_launch_A("l7a", D, ffnL1)
    in7 = [{"x": r5b[c]["x_up"], "o_in": np.ascontiguousarray(Ob[c // 2][:, (c % 2) * TC:(c % 2 + 1) * TC]),
            "woT": woT_l1a, "wg": wgL1, "wu": wuL1, "gsc": gscL1, "usc": uscL1}
           for c in range(NCORE)]
    r7a = _run(nc, in7)

    wdL1 = _pretile_down(np.ascontiguousarray(inputs["L1_ffn_down_w"].T), 5632)
    dscL1 = np.ascontiguousarray(inputs["L1_ffn_down_s"].reshape(16, 128).T).astype(f32)
    wgk1_pad = np.zeros((D, 128), f32)
    wgk1_pad[:, 0:16] = inputs["L1_attn_b_Wgk1"]
    wcat7 = np.concatenate([
        _pretile(np.ascontiguousarray(inputs["L1_attn_b_Wq"])),
        _pretile(np.ascontiguousarray(inputs["L1_attn_b_Wk"])),
        _pretile(np.ascontiguousarray(inputs["L1_attn_b_Wv"])),
        _pretile(np.ascontiguousarray(inputs["L1_attn_b_Wg"])),
        _pretile(wgk1_pad)], axis=0)
    proj7 = dict(n=49, kinds=(["plain"] * 48 + ["gkt1"]),
                 outs=[("qg", 16), ("kg", 16), ("vg", 8), ("gg8", 8)], gk=True)
    nc = build_launch_B("l7b", dict(I=5632, dt=F32, offset128=False), proj7)
    in7b = [{"x": r7a[c]["x_up"], "h": r7a[c]["h"], "wd": wdL1, "dsc": dscL1, "wcat": wcat7,
             "wgk2": inputs["L1_attn_b_Wgk2"].astype(f32),
             "bgk2": inputs["L1_attn_b_bgk2"][None, :].astype(f32)}
            for c in range(NCORE)]
    r7b = _run(nc, in7b)

    Qg = _halves([r7b[c]["qg"] for c in range(NCORE)])
    Kg = _halves([r7b[c]["kg"] for c in range(NCORE)])
    Vg = _halves([r7b[c]["vg"] for c in range(NCORE)])
    Gg = _halves([r7b[c]["gg8"] for c in range(NCORE)])
    GKt = [np.concatenate([r7b[2 * b]["gk_ls"], r7b[2 * b + 1]["gk_ls"]], axis=0) for b in range(B)]
    tril_inc = np.tril(np.ones((64, 64), f32))
    tril_inc = (np.arange(64)[:, None] <= np.arange(64)[None, :]).astype(f32)
    ustrict = (np.arange(64)[:, None] > np.arange(64)[None, :]).astype(f32)
    maskA = (np.arange(64)[:, None] <= np.arange(64)[None, :]).astype(NPBF)
    wn_bc = np.ascontiguousarray(
        np.broadcast_to(inputs["L1_attn_b_Wnorm"].astype(f32), (128, 256)))
    nc = build_gla_launch("l8")
    in8 = []
    for c in range(NCORE):
        b, hg = c // 2, c % 2
        sl = slice(hg * 1024, (hg + 1) * 1024)
        sv = slice(hg * 512, (hg + 1) * 512)
        in8.append({"qf8": np.ascontiguousarray(Qg[b][sl]),
                    "kf8": np.ascontiguousarray(Kg[b][sl]),
                    "kt8": np.ascontiguousarray(Kg[b][sl].T),
                    "vt8": np.ascontiguousarray(Vg[b][sv].T),
                    "gkt": np.ascontiguousarray(GKt[b][:, sl]),
                    "gt": np.ascontiguousarray(Gg[b][sv].T),
                    "wnorm_bc": wn_bc, "trilm": tril_inc, "ustrict": ustrict,
                    "maskA": maskA})
    r8 = _run(nc, in8)

    woT_gla = _pretile(np.ascontiguousarray(inputs["L1_attn_b_Wo"]).astype(f32))
    OGb = [np.concatenate([r8[2 * b]["og8"], r8[2 * b + 1]["og8"]], axis=1) for b in range(B)]
    nc = build_launch_A("l9a", 1024, ffnL1)
    in9 = [{"x": r7b[c]["x_up"],
            "o_in": np.ascontiguousarray(OGb[c // 2][(c % 2) * TC:(c % 2 + 1) * TC, :].T),
            "woT": woT_gla, "wg": wgL1, "wu": wuL1, "gsc": gscL1, "usc": uscL1}
           for c in range(NCORE)]
    r9a = _run(nc, in9)

    nc = build_launch_B("l9b", dict(I=5632, dt=F32, offset128=False, final=True), None)
    in9b = [{"x": r9a[c]["x_up"], "h": r9a[c]["h"], "wd": wdL1, "dsc": dscL1}
            for c in range(NCORE)]
    r9b = _run(nc, in9b)

    out = np.empty((B, S, D), np.float32)
    for c in range(NCORE):
        b, hf = c // 2, c % 2
        out[b, hf * TC:(hf + 1) * TC, :] = r9b[c]["x_fin"].T
    return out


def _pretile(Wt):
    din, dout = Wt.shape
    assert dout % 128 == 0
    blocks = Wt.reshape(din, dout // 128, 128).transpose(1, 0, 2).reshape(dout // 128, din * 128)
    return np.ascontiguousarray(blocks)


def _pretile_down(WdT, I):
    qh = I // 4
    blocks = []
    for d in range(16):
        for q in range(4):
            blocks.append(np.ascontiguousarray(
                WdT[q * qh:(q + 1) * qh, d * 128:(d + 1) * 128]).reshape(-1))
    return np.ascontiguousarray(np.stack(blocks, axis=0))


